# version 03
# baseline (speedup 1.0000x reference)
import numpy as np
from collections import defaultdict

import concourse.bass as bass
import concourse.mybir as mybir
from concourse.bass_utils import run_bass_kernel_spmd

F32 = mybir.dt.float32
BF16 = mybir.dt.bfloat16
AF = mybir.ActivationFunctionType
ALU = mybir.AluOpType
AX = mybir.AxisListType

D = 2048
DC = 16
HC = 11
NE = 8
NH = 8
DK = 64
DV = 512
QC = 12
W = 1024
NCHUNK = 4
CORE_T = W * NCHUNK
TOK = 2 * 16384
EPS = 1e-5
ASCALE = 0.125
WF = 1026


def _groups(c0, c1, gmax=512):
    w = c1 - c0
    n = -(-w // gmax)
    out, base = [], c0
    for i in range(n):
        gw = (w + n - 1 - i) // n
        out.append((base, base + gw))
        base += gw
    assert base == c1
    return out


def _tiles128(c0, c1):
    out, b = [], c0
    while b < c1:
        out.append((b, min(b + 128, c1)))
        b = min(b + 128, c1)
    return out


class Emitter:
    COMPUTE = ("pe", "act", "dve", "pool", "sp")

    def __init__(self, nc):
        self.nc = nc
        self.eng = {"pe": nc.tensor, "act": nc.scalar, "dve": nc.vector,
                    "pool": nc.gpsimd, "sp": nc.sync}
        self.dma_issuer = {"dmaS": "sp", "dmaG": "pool", "dmaA": "act"}
        self.sem = {}
        self.count = defaultdict(int)
        for t in self.COMPUTE:
            self.sem[t] = nc.alloc_semaphore(f"sem_{t}")
        self.known = {e: defaultdict(int) for e in self.COMPUTE}
        self.writer = {}
        self.readers = {}
        self.stats = defaultdict(int)

    def _need(self, eng, track, val):
        if val <= 0 or self.known[eng][track] >= val:
            return
        self.eng[eng].wait_ge(self.sem[track], val)
        self.stats["waits"] += 1
        self.known[eng][track] = val

    def _dep_in(self, eng, reads, writes, pe_exempt):
        for k in reads:
            w = self.writer.get(k)
            if w is not None:
                t, v = w
                if t == eng == "pe" and pe_exempt:
                    continue
                if t == "pe" and v > self.count["pe"]:
                    raise RuntimeError(f"read {k}: PE val {v} not inc'd")
                self._need(eng, t, v)
        for k in writes:
            w = self.writer.get(k)
            if w is not None:
                t, v = w
                if not (t == eng == "pe" and pe_exempt):
                    if t == "pe" and v > self.count["pe"]:
                        raise RuntimeError(f"WAW {k}: PE val {v} not inc'd")
                    self._need(eng, t, v)
            for t, v in self.readers.get(k, {}).items():
                if t == eng == "pe" and pe_exempt:
                    continue
                if t == "pe" and v > self.count["pe"]:
                    raise RuntimeError(f"WAR {k}: PE val {v} not inc'd")
                self._need(eng, t, v)

    def _dep_out(self, track, myval, reads, writes):
        for k in writes:
            self.writer[k] = (track, myval)
            self.readers[k] = {}
        for k in reads:
            self.readers.setdefault(k, {})
            prev = self.readers[k].get(track, 0)
            self.readers[k][track] = max(prev, myval)

    def op(self, eng, emit, reads=(), writes=(), inc=True, pe_exempt=True):
        self._dep_in(eng, reads, writes, pe_exempt)
        ins = emit(self.eng[eng])
        self.stats["ops_" + eng] += 1
        if inc:
            self.count[eng] += 1
            ins.then_inc(self.sem[eng], 1)
            myval = self.count[eng]
        else:
            assert eng == "pe"
            myval = self.count[eng] + 1
        self._dep_out(eng, myval, reads, writes)
        return ins

    def dma(self, queue, out_ap, in_ap, reads=(), writes=(), stream=None,
            group_val=None, **kw):
        issuer = self.dma_issuer[queue]
        assert stream is not None
        if stream not in self.sem:
            self.sem[stream] = self.nc.alloc_semaphore(f"sem_{stream}")
        self._dep_in(issuer, reads, writes, False)
        ins = self.eng[issuer].dma_start(out=out_ap, in_=in_ap, **kw)
        self.count[stream] += 16
        ins.then_inc(self.sem[stream], 16)
        self.stats["dma_" + stream[:2]] += 1
        v = self.count[stream] if group_val is None else group_val
        self._dep_out(stream, v, reads, writes)
        return ins

    def barrier_track(self, eng, track, val=None):
        v = self.count[track] if val is None else val
        self._need(eng, track, v)

    def fence(self):
        tracks = [t for t in self.count.keys()]
        for e in self.COMPUTE:
            for t in tracks:
                if t != e:
                    self._need(e, t, self.count[t])
        self.stats["fences"] += 1


def build_program(phases=("ffn1", "attn", "ffn3"), nchunk=NCHUNK):
    nc = bass.Bass("TRN2", target_bir_lowering=False)

    xT = nc.dram_tensor("xT", [D, CORE_T + 2], F32, kind="ExternalInput")
    gupB = nc.dram_tensor("gupB", [NE, 2 * HC, 128, DC, 128], F32, kind="ExternalInput")
    downB = nc.dram_tensor("downB", [NE, DC, 128, HC, 128], F32, kind="ExternalInput")
    convB = nc.dram_tensor("convB", [3, QC, 128, DC, 128], F32, kind="ExternalInput")
    projB = nc.dram_tensor("projB", [DC, 128, 4, 128], F32, kind="ExternalInput")
    routT = nc.dram_tensor("routT", [D, NE], F32, kind="ExternalInput")
    wq2 = nc.dram_tensor("wq2", [DV, 512], F32, kind="ExternalInput")
    wk2 = nc.dram_tensor("wk2", [DV, 512], F32, kind="ExternalInput")
    wv2 = nc.dram_tensor("wv2", [DV, 512], F32, kind="ExternalInput")
    wo2 = nc.dram_tensor("wo2", [DV, 512], F32, kind="ExternalInput")
    cbias = nc.dram_tensor("cbias", [128, QC], F32, kind="ExternalInput")
    pbias = nc.dram_tensor("pbias", [128, DC], F32, kind="ExternalInput")
    bqc = nc.dram_tensor("bqc", [64, NH], F32, kind="ExternalInput")
    bkc = nc.dram_tensor("bkc", [64, NH], F32, kind="ExternalInput")
    bvc = nc.dram_tensor("bvc", [128, 4], F32, kind="ExternalInput")
    boc = nc.dram_tensor("boc", [128, 4], F32, kind="ExternalInput")
    outT = nc.dram_tensor("outT", [D, CORE_T], F32, kind="ExternalOutput")

    x_sb = nc.alloc_sbuf_tensor("x_sb", [128, DC, WF], F32)
    hx = nc.alloc_sbuf_tensor("hx", [128, DC, WF], BF16)
    slab = [nc.alloc_sbuf_tensor(f"slab{i}", [128, DC, 128], BF16) for i in range(4)]
    dslab = [nc.alloc_sbuf_tensor(f"dslab{i}", [128, HC, 128], BF16) for i in range(2)]
    wq_sb = nc.alloc_sbuf_tensor("wq_sb", [128, 4, 512], BF16)
    wk_sb = nc.alloc_sbuf_tensor("wk_sb", [128, 4, 512], BF16)
    wv_sb = nc.alloc_sbuf_tensor("wv_sb", [128, 4, 512], BF16)
    wo_sb = nc.alloc_sbuf_tensor("wo_sb", [128, 4, 512], BF16)
    rout_sb = nc.alloc_sbuf_tensor("rout_sb", [128, DC, NE], BF16)
    cb_b = nc.alloc_sbuf_tensor("cb_b", [128, QC], F32)
    pb_b = nc.alloc_sbuf_tensor("pb_b", [128, DC], F32)
    bq_b = nc.alloc_sbuf_tensor("bq_b", [64, NH], F32)
    bk_b = nc.alloc_sbuf_tensor("bk_b", [64, NH], F32)
    bv_bf = nc.alloc_sbuf_tensor("bv_bf", [128, 4], BF16)
    bo_b = nc.alloc_sbuf_tensor("bo_b", [128, 4], F32)
    bo2_b = nc.alloc_sbuf_tensor("bo2_b", [128, 4], F32)
    ones_c = nc.alloc_sbuf_tensor("ones_c", [128, 1], BF16)
    ones_r = nc.alloc_sbuf_tensor("ones_r", [1, 128], F32)
    identf = nc.alloc_sbuf_tensor("identf", [128, 128], F32)
    onehot = nc.alloc_sbuf_tensor("onehot", [8, NE, 128], F32)
    eps_t = nc.alloc_sbuf_tensor("eps_t", [1, 1], F32)
    rows = nc.alloc_sbuf_tensor("rows", [1, 2048], F32)
    halo_sb = nc.alloc_sbuf_tensor("halo_sb", [128, DC, 2], BF16)
    sq_sb = [nc.alloc_sbuf_tensor(f"sq{i}", [128, 512], BF16) for i in range(2)]
    lg_sb = nc.alloc_sbuf_tensor("lg_sb", [128, 9, 8], F32)
    comb_sb = nc.alloc_sbuf_tensor("comb_sb", [128, 9, 8], F32)
    wcols = nc.alloc_sbuf_tensor("wcols", [128, 6], F32)
    eqs = nc.alloc_sbuf_tensor("eqs", [128, 3, 8], F32)
    q_sb = nc.alloc_sbuf_tensor("q_sb", [128, 4, W], BF16)
    k_sb = nc.alloc_sbuf_tensor("k_sb", [128, 4, W], BF16)
    t_sb = nc.alloc_sbuf_tensor("t_sb", [128, HC, WF], BF16)
    silu_sb = [nc.alloc_sbuf_tensor(f"silu{i}", [128, 512], F32) for i in range(3)]
    cb_sb = nc.alloc_sbuf_tensor("cb_sb", [128, 1056], F32)
    lgfm = nc.alloc_sbuf_tensor("lgfm", [8, 1056], F32)
    comb_fm = nc.alloc_sbuf_tensor("comb_fm", [8, 1056], F32)
    _base = nc.lookup_mloc(t_sb).addr
    _end = nc.lookup_mloc(comb_fm).addr + 1056 * 4

    def _at(name, shape, dtype, off):
        nonlocal_end = off + int(np.prod(shape[1:])) * mybir.dt.size(dtype)
        assert _base + nonlocal_end <= _end, (name, nonlocal_end, _end - _base)
        return nc.alloc_sbuf_tensor_at(name, shape, dtype, offset=_base + off)

    v_sb = _at("v_sb", [128, 4, W], BF16, 0)
    vh_sb = _at("vh_sb", [128, 8, NH, 65], BF16, 8192)
    qh_sb = _at("qh_sb", [64, W], BF16, 16512)
    kh_sb = _at("kh_sb", [64, W], BF16, 18560)
    expT = [_at(f"expT{i}", [128, W], BF16, 20608 + 2048 * i) for i in range(2)]
    ctx_all = _at("ctx_all", [128, 4, W], BF16, 24704)
    ns_sb = _at("ns_sb", [64, W], F32, 32896)
    mha_sb = nc.alloc_sbuf_tensor_at("mha_sb", [128, 4, W], BF16,
                                     offset=nc.lookup_mloc(q_sb).addr)

    pb = [nc.alloc_psum_tensor(f"pb{i}", [128, 512], F32) for i in range(8)]
    PB = [f"pb{i}" for i in range(8)]

    em = Emitter(nc)

    em.op("dve", lambda e: e.memset(ones_c[:, :], 1.0), writes=["ones_c"])
    em.op("dve", lambda e: e.memset(ones_r[:, :], 1.0), writes=["ones_r"])
    em.op("dve", lambda e: e.memset(eps_t[:, :], EPS), writes=["eps_t"])
    em.op("pool", lambda e: e.memset(identf[:, :], 1.0), writes=["identf"])
    em.op("pool", lambda e: e.affine_select(out=identf[:, :], in_=identf[:, :],
            pattern=[[-1, 128]], compare_op=ALU.is_equal, fill=0.0, base=0,
            channel_multiplier=1), reads=["identf"], writes=["identf"])
    em.op("pool", lambda e: e.memset(onehot.ap().rearrange("p a b -> p (a b)"), 1.0),
          writes=["onehot"])
    em.op("pool", lambda e: e.affine_select(
            out=onehot.ap().rearrange("p a b -> p a b"),
            in_=onehot.ap().rearrange("p a b -> p a b"),
            pattern=[[-1, NE], [0, 128]], compare_op=ALU.is_equal, fill=0.0,
            base=0, channel_multiplier=1), reads=["onehot"], writes=["onehot"])
    em.dma("dmaG", wq_sb.ap(),
           wq2.ap().rearrange("(a p) b -> p a b", p=128), writes=["wq_sb"],
           stream="initG", group_val=96)
    em.dma("dmaG", wk_sb.ap(),
           wk2.ap().rearrange("(a p) b -> p a b", p=128), writes=["wk_sb"],
           stream="initG", group_val=96)
    em.dma("dmaG", wv_sb.ap(),
           wv2.ap().rearrange("(a p) b -> p a b", p=128), writes=["wv_sb"],
           stream="initG", group_val=96)
    em.dma("dmaG", wo_sb.ap(),
           wo2.ap().rearrange("(a p) b -> p a b", p=128), writes=["wo_sb"],
           stream="initG", group_val=96)
    em.dma("dmaG", rout_sb.ap(),
           routT.ap().rearrange("(a p) b -> p a b", p=128), writes=["rout_sb"],
           stream="initG", group_val=96)
    em.dma("dmaG", bv_bf[:, :], bvc.ap(), writes=["bv_bf"],
           stream="initG", group_val=96)
    em.dma("dmaS", cb_b[:, :], cbias.ap(), writes=["cb_b"],
           stream="initS", group_val=80)
    em.dma("dmaS", pb_b[:, :], pbias.ap(), writes=["pb_b"],
           stream="initS", group_val=80)
    em.dma("dmaS", bq_b[:, :], bqc.ap(), writes=["bq_b"],
           stream="initS", group_val=80)
    em.dma("dmaS", bk_b[:, :], bkc.ap(), writes=["bk_b"],
           stream="initS", group_val=80)
    em.dma("dmaS", bo_b[:, :], boc.ap(), writes=["bo_b"],
           stream="initS", group_val=80)

    for m in range(4):
        for kk in range(4):
            em.op("pe", lambda e, m=m, kk=kk: e.matmul(
                pb[6][0:128, m:m + 1], wo_sb[:, kk, m * 128:(m + 1) * 128],
                bv_bf[:, kk:kk + 1], start=(m == 0 and kk == 0),
                stop=(m == 3 and kk == 3)),
                reads=["wo_sb", "bv_bf"], writes=[PB[6]],
                inc=(m == 3 and kk == 3))
    em.op("dve", lambda e: e.tensor_add(bo2_b[:, :], bo_b[:, :], pb[6][0:128, 0:4]),
          reads=["bo_b", PB[6]], writes=["bo2_b"])

    def ln(f0, f1):
        gs = _groups(f0, f1)
        for gi, (g0, g1) in enumerate(gs):
            gw = g1 - g0
            bs, bq = 2 * gi, 2 * gi + 1
            for dc in range(DC):
                em.op("act", lambda e, dc=dc, g0=g0, g1=g1: e.copy(
                    out=hx[:, dc, g0:g1], in_=x_sb[:, dc, g0:g1]),
                    reads=[("x", dc)], writes=[("hx", dc)])
                em.op("pool", lambda e, dc=dc, g0=g0, g1=g1, gw=gw: e.tensor_mul(
                    sq_sb[dc % 2][:, 0:gw], hx[:, dc, g0:g1], hx[:, dc, g0:g1]),
                    reads=[("hx", dc)], writes=[("sq", dc % 2)])
                em.op("pe", lambda e, dc=dc, g0=g0, g1=g1, gw=gw, bs=bs: e.matmul(
                    pb[bs][0:1, 0:gw], ones_c[:, :], hx[:, dc, g0:g1],
                    start=(dc == 0), stop=(dc == DC - 1)),
                    reads=[("hx", dc), "ones_c"], writes=[PB[bs]])
                em.op("pe", lambda e, dc=dc, gw=gw, bq=bq: e.matmul(
                    pb[bq][0:1, 0:gw], ones_c[:, :], sq_sb[dc % 2][:, 0:gw],
                    start=(dc == 0), stop=(dc == DC - 1)),
                    reads=[("sq", dc % 2), "ones_c"], writes=[PB[bq]])
            em.op("act", lambda e, gw=gw, bs=bs: e.mul(
                rows[0:1, 0:gw], pb[bs][0:1, 0:gw], 1.0 / D),
                reads=[PB[bs]], writes=["rA"])
            em.op("act", lambda e, gw=gw, bq=bq: e.mul(
                rows[0:1, 512:512 + gw], pb[bq][0:1, 0:gw], 1.0 / D),
                reads=[PB[bq]], writes=["rB"])
            em.op("dve", lambda e, gw=gw: e.tensor_mul(rows[0:1, 1024:1024 + gw],
                  rows[0:1, 0:gw], rows[0:1, 0:gw]), reads=["rA"], writes=["rC"])
            em.op("dve", lambda e, gw=gw: e.tensor_sub(rows[0:1, 512:512 + gw],
                  rows[0:1, 512:512 + gw], rows[0:1, 1024:1024 + gw]),
                  reads=["rB", "rC"], writes=["rB"])
            em.op("act", lambda e, gw=gw: e.activation(out=rows[0:1, 1024:1024 + gw],
                  in_=rows[0:1, 512:512 + gw], func=AF.Ln, bias=eps_t[0:1, :]),
                  reads=["rB", "eps_t"], writes=["rC"])
            em.op("act", lambda e, gw=gw: e.activation(out=rows[0:1, 1024:1024 + gw],
                  in_=rows[0:1, 1024:1024 + gw], func=AF.Exp, scale=-0.5),
                  reads=["rC"], writes=["rC"])
            em.op("pe", lambda e, gw=gw, bs=bs: e.matmul(pb[bs][0:128, 0:gw],
                  ones_r[:, :], rows[0:1, 0:gw], start=True, stop=True),
                  reads=["rA", "ones_r"], writes=[PB[bs]])
            em.op("pe", lambda e, gw=gw, bq=bq: e.matmul(pb[bq][0:128, 0:gw],
                  ones_r[:, :], rows[0:1, 1024:1024 + gw], start=True, stop=True),
                  reads=["rC", "ones_r"], writes=[PB[bq]])
        for gi, (g0, g1) in enumerate(gs):
            gw = g1 - g0
            bs, bq = 2 * gi, 2 * gi + 1
            for dc in range(DC):
                em.op("dve", lambda e, dc=dc, g0=g0, g1=g1, gw=gw, bs=bs: e.tensor_sub(
                    hx[:, dc, g0:g1], x_sb[:, dc, g0:g1], pb[bs][0:128, 0:gw]),
                    reads=[("x", dc), PB[bs]], writes=[("hx", dc)])
                em.op("dve", lambda e, dc=dc, g0=g0, g1=g1, gw=gw, bq=bq: e.tensor_mul(
                    hx[:, dc, g0:g1], hx[:, dc, g0:g1], pb[bq][0:128, 0:gw]),
                    reads=[("hx", dc), PB[bq]], writes=[("hx", dc)])

    def ffn(f0, f1):
        gs = _groups(f0, f1)
        ng = len(gs)
        tiles = _tiles128(f0, f1)

        def router():
            for gi, (g0, g1) in enumerate(gs):
                gw = g1 - g0
                for kk in range(DC):
                    em.op("pe", lambda e, kk=kk, g0=g0, g1=g1, gw=gw: e.matmul(
                        pb[6][0:8, 0:gw], rout_sb[:, kk, :], hx[:, kk, g0:g1],
                        start=(kk == 0), stop=(kk == DC - 1)),
                        reads=[("hx", kk), "rout_sb"], writes=[PB[6]],
                        inc=(kk == DC - 1))
                em.op("act", lambda e, g0=g0, g1=g1, gw=gw: e.copy(
                    out=lgfm[0:8, g0:g1], in_=pb[6][0:8, 0:gw]),
                    reads=[PB[6]], writes=["lgfm"])

        def routing_tiles():
            for ti, (t0, t1) in enumerate(tiles):
                tw = t1 - t0
                em.op("pe", lambda e, t0=t0, t1=t1, tw=tw: e.transpose(
                    pb[7][0:tw, 0:8], lgfm[0:8, t0:t1], identf[0:8, 0:8]),
                    reads=["lgfm", "identf"], writes=[PB[7]])
                em.op("act", lambda e, ti=ti, tw=tw: e.copy(
                    out=lg_sb[0:tw, ti, :], in_=pb[7][0:tw, 0:8]),
                    reads=[PB[7]], writes=[("lg", ti)])
                P = slice(0, tw)
                em.op("dve", lambda e, ti=ti, P=P: e.reduce_max(
                    out=wcols[P, 0:1], in_=lg_sb[P, ti, :], axis=AX.X),
                    reads=[("lg", ti)], writes=["wc"])
                em.op("dve", lambda e, ti=ti, P=P: e.tensor_scalar(
                    out=eqs[P, 0, :], in0=lg_sb[P, ti, :], scalar1=wcols[P, 0:1],
                    scalar2=None, op0=ALU.is_equal),
                    reads=[("lg", ti), "wc"], writes=["eq0"])
                em.op("dve", lambda e, P=P: e.tensor_scalar(
                    out=eqs[P, 1, :], in0=eqs[P, 0, :], scalar1=-2000.0,
                    scalar2=None, op0=ALU.mult), reads=["eq0"], writes=["eq1"])
                em.op("dve", lambda e, ti=ti, P=P: e.tensor_add(
                    eqs[P, 1, :], eqs[P, 1, :], lg_sb[P, ti, :]),
                    reads=[("lg", ti), "eq1"], writes=["eq1"])
                em.op("dve", lambda e, P=P: e.reduce_max(
                    out=wcols[P, 1:2], in_=eqs[P, 1, :], axis=AX.X),
                    reads=["eq1"], writes=["wc"])
                em.op("dve", lambda e, P=P: e.tensor_sub(
                    wcols[P, 2:3], wcols[P, 1:2], wcols[P, 0:1]),
                    reads=["wc"], writes=["wc"])
                em.op("act", lambda e, P=P: e.activation(
                    out=wcols[P, 2:3], in_=wcols[P, 2:3], func=AF.Exp),
                    reads=["wc"], writes=["wc"])
                em.op("act", lambda e, P=P: e.add(wcols[P, 3:4], wcols[P, 2:3], 1.0),
                      reads=["wc"], writes=["wc"])
                em.op("dve", lambda e, P=P: e.reciprocal(
                    out=wcols[P, 4:5], in_=wcols[P, 3:4]), reads=["wc"], writes=["wc"])
                em.op("dve", lambda e, P=P: e.tensor_scalar(
                    out=wcols[P, 5:6], in0=wcols[P, 4:5], scalar1=0.5, scalar2=None,
                    op0=ALU.mult), reads=["wc"], writes=["wc"])
                em.op("dve", lambda e, P=P: e.tensor_mul(
                    wcols[P, 2:3], wcols[P, 2:3], wcols[P, 5:6]),
                    reads=["wc"], writes=["wc"])
                em.op("dve", lambda e, P=P: e.tensor_scalar(
                    out=eqs[P, 2, :], in0=eqs[P, 1, :], scalar1=wcols[P, 1:2],
                    scalar2=None, op0=ALU.is_equal), reads=["eq1", "wc"], writes=["eq2"])
                em.op("dve", lambda e, P=P: e.tensor_scalar(
                    out=eqs[P, 0, :], in0=eqs[P, 0, :], scalar1=wcols[P, 5:6],
                    scalar2=None, op0=ALU.mult), reads=["eq0", "wc"], writes=["eq0"])
                em.op("dve", lambda e, P=P: e.tensor_scalar(
                    out=eqs[P, 2, :], in0=eqs[P, 2, :], scalar1=wcols[P, 2:3],
                    scalar2=None, op0=ALU.mult), reads=["eq2", "wc"], writes=["eq2"])
                em.op("dve", lambda e, ti=ti, P=P: e.tensor_add(
                    comb_sb[P, ti, :], eqs[P, 0, :], eqs[P, 2, :]),
                    reads=["eq0", "eq2"], writes=[("comb", ti)])

        def comb_transposes():
            for ti, (t0, t1) in enumerate(tiles):
                tw = t1 - t0
                em.op("pe", lambda e, ti=ti, tw=tw: e.transpose(
                    pb[7][0:8, 0:tw], comb_sb[0:tw, ti, :], identf[0:tw, 0:tw]),
                    reads=[("comb", ti), "identf"], writes=[PB[7]])
                em.op("act", lambda e, t0=t0, t1=t1, tw=tw: e.copy(
                    out=comb_fm[0:8, t0:t1], in_=pb[7][0:8, 0:tw]),
                    reads=[PB[7]], writes=["comb_fm"])

        def cb_bcast(e_, cbanks):
            for gi, (g0, g1) in enumerate(gs):
                gw = g1 - g0
                cbk = cbanks[gi]
                em.op("pe", lambda e, e_=e_, g0=g0, g1=g1, gw=gw, cbk=cbk: e.matmul(
                    pb[cbk][0:128, 0:gw], onehot[:, e_, :], comb_fm[0:8, g0:g1],
                    start=True, stop=True),
                    reads=["comb_fm", "onehot"], writes=[PB[cbk]])
                em.op("act", lambda e, g0=g0, g1=g1, gw=gw, cbk=cbk: e.copy(
                    out=cb_sb[:, g0:g1], in_=pb[cbk][0:128, 0:gw]),
                    reads=[PB[cbk]], writes=["cb"])

        gq = [(e_, jj) for e_ in range(NE) for jj in range(2 * HC)]
        gq_next = [0]

        def gup_fetch(upto):
            while gq_next[0] < min(upto, len(gq)):
                i = gq_next[0]
                e_, jj = gq[i]
                em.dma("dmaG", slab[i % 4].ap(), gupB.ap()[e_, jj],
                       writes=[("slab", i % 4)], stream=f"sl{i % 4}")
                gq_next[0] += 1

        dq = [(e_, m) for e_ in range(NE) for m in range(DC)]
        dq_next = [0]

        def down_fetch(upto):
            while dq_next[0] < min(upto, len(dq)):
                i = dq_next[0]
                e_, m = dq[i]
                em.dma("dmaG", dslab[i % 2].ap(), downB.ap()[e_, m],
                       writes=[("dslab", i % 2)], stream=f"dsl{i % 2}")
                dq_next[0] += 1

        def gu_banks(j2):
            if ng == 3:
                return [0, 1, 2], [3, 4, 5]
            if j2 % 2 == 0:
                return [0, 1], [2, 3]
            return [4, 5], [6, 7]

        def y_banks(m):
            if ng == 3:
                return [0, 1, 2]
            return [0, 1] if m % 2 == 0 else [2, 3]

        cbanks = [7, 7, 7] if ng == 3 else [6, 7]

        def gup_mms(e_, j2):
            ig = e_ * 2 * HC + 2 * j2
            gup_fetch(ig + 4)
            sg_, su_ = slab[ig % 4], slab[(ig + 1) % 4]
            kg, ku = ("slab", ig % 4), ("slab", (ig + 1) % 4)
            gb, ub = gu_banks(j2)
            for gi, (g0, g1) in enumerate(gs):
                gw = g1 - g0
                for kk in range(DC):
                    em.op("pe", lambda e, kk=kk, g0=g0, g1=g1, gw=gw, bk=gb[gi],
                          sg_=sg_: e.matmul(
                        pb[bk][0:128, 0:gw], sg_[:, kk, :], hx[:, kk, g0:g1],
                        start=(kk == 0), stop=(kk == DC - 1)),
                        reads=[kg, ("hx", kk)], writes=[PB[gb[gi]]],
                        inc=(kk == DC - 1))
                for kk in range(DC):
                    em.op("pe", lambda e, kk=kk, g0=g0, g1=g1, gw=gw, bk=ub[gi],
                          su_=su_: e.matmul(
                        pb[bk][0:128, 0:gw], su_[:, kk, :], hx[:, kk, g0:g1],
                        start=(kk == 0), stop=(kk == DC - 1)),
                        reads=[ku, ("hx", kk)], writes=[PB[ub[gi]]],
                        inc=(kk == DC - 1))

        def silu_evac(j2):
            gb, ub = gu_banks(j2)
            for gi, (g0, g1) in enumerate(gs):
                gw = g1 - g0
                em.op("act", lambda e, gi=gi, gw=gw, bk=gb[gi]: e.activation(
                    out=silu_sb[gi][:, 0:gw], in_=pb[bk][0:128, 0:gw],
                    func=AF.Silu), reads=[PB[gb[gi]]], writes=[("silu", gi)])

        def t_evac(j2):
            gb, ub = gu_banks(j2)
            for gi, (g0, g1) in enumerate(gs):
                gw = g1 - g0
                em.op("dve", lambda e, gi=gi, g0=g0, g1=g1, gw=gw: e.tensor_mul(
                    silu_sb[gi][:, 0:gw], silu_sb[gi][:, 0:gw], cb_sb[:, g0:g1]),
                    reads=[("silu", gi), "cb"], writes=[("silu", gi)])
                em.op("dve", lambda e, j2=j2, gi=gi, g0=g0, g1=g1, gw=gw,
                      bk=ub[gi]: e.tensor_mul(
                    t_sb[:, j2, g0:g1], silu_sb[gi][:, 0:gw],
                    pb[bk][0:128, 0:gw]),
                    reads=[("silu", gi), PB[ub[gi]]], writes=[("t", j2)])

        def down_phase(e_):
            for m in range(DC):
                idn = e_ * DC + m
                down_fetch(idn + 2)
                ds_ = dslab[idn % 2]
                kd = ("dslab", idn % 2)
                yb = y_banks(m)
                for gi, (g0, g1) in enumerate(gs):
                    gw = g1 - g0
                    for k2 in range(HC):
                        em.op("pe", lambda e, k2=k2, g0=g0, g1=g1, gw=gw,
                              bk=yb[gi], ds_=ds_: e.matmul(
                            pb[bk][0:128, 0:gw], ds_[:, k2, :], t_sb[:, k2, g0:g1],
                            start=(k2 == 0), stop=(k2 == HC - 1)),
                            reads=[kd, ("t", k2)], writes=[PB[yb[gi]]],
                            inc=(k2 == HC - 1))
                    ybk = yb[gi]
                    em.op("dve", lambda e, m=m, g0=g0, g1=g1, gw=gw, bk=ybk:
                          e.tensor_add(x_sb[:, m, g0:g1], x_sb[:, m, g0:g1],
                                       pb[bk][0:128, 0:gw]),
                          reads=[("x", m), PB[ybk]], writes=[("x", m)])

        gup_fetch(4)
        router()
        routing_tiles()
        for e_ in range(NE):
            if e_ == 0 and ng == 2:
                gup_mms(0, 0)
                silu_evac(0)
                gup_mms(0, 1)
                comb_transposes()
                cb_bcast(0, [0, 1])
                t_evac(0)
                silu_evac(1)
                t_evac(1)
                start_j2 = 2
            else:
                if e_ == 0:
                    comb_transposes()
                cb_bcast(e_, cbanks)
                start_j2 = 0
            for j2 in range(start_j2, HC):
                gup_mms(e_, j2)
                silu_evac(j2)
                t_evac(j2)
            down_phase(e_)

    def attn(c):
        BG = [(0, 512), (512, 1024)]
        em.op("dve", lambda e: e.memset(vh_sb[:, :, :, 64:65], 1.0),
              writes=["vh_ones"])
        cq = [(w_, j) for j in range(QC) for w_ in range(3)]
        cq_next = [0]

        def conv_fetch(upto):
            while cq_next[0] < min(upto, len(cq)):
                i = cq_next[0]
                w_, j = cq[i]
                em.dma("dmaG", slab[i % 4].ap(), convB.ap()[w_, j],
                       writes=[("slab", i % 4)], stream=f"sl{i % 4}")
                cq_next[0] += 1

        for j in range(QC):
            conv_fetch(3 * j + 4)
            for gi, (g0, g1) in enumerate(BG):
                bank = 2 * (j % 2) + gi
                first = True
                for w_ in range(3):
                    i = 3 * j + w_
                    ks = ("slab", i % 4)
                    sl = slab[i % 4]
                    for kk in range(DC):
                        em.op("pe", lambda e, kk=kk, w_=w_, g0=g0, bank=bank,
                              sl=sl, first=first: e.matmul(
                            pb[bank][0:128, 0:512], sl[:, kk, :],
                            hx[:, kk, g0 + w_:g0 + w_ + 512],
                            start=first, stop=(w_ == 2 and kk == DC - 1)),
                            reads=[ks, ("hx", kk)], writes=[PB[bank]],
                            inc=(w_ == 2 and kk == DC - 1))
                        first = False
                dst = (q_sb, j) if j < 4 else ((k_sb, j - 4) if j < 8 else (v_sb, j - 8))
                dt_, dj = dst
                dname = "q_sb" if j < 4 else ("k_sb" if j < 8 else "v_sb")
                em.op("act", lambda e, dt_=dt_, dj=dj, j=j, g0=g0, g1=g1,
                      bank=bank: e.activation(
                    out=dt_[:, dj, g0:g1], in_=pb[bank][0:128, 0:512],
                    func=AF.Identity, bias=cb_b[:, j:j + 1]),
                    reads=[PB[bank], "cb_b"], writes=[(dname, dj)])
        for tc in range(8):
            for dv in range(4):
                for h in range(NH):
                    em.op("pe", lambda e, tc=tc, dv=dv, h=h: e.matmul(
                        pb[6][0:128, h * 64:(h + 1) * 64],
                        v_sb[:, dv, tc * 128:(tc + 1) * 128],
                        wv_sb[:, dv, h * 64:(h + 1) * 64],
                        start=(dv == 0 and h == 0), stop=(dv == 3 and h == 7)),
                        reads=[("v_sb", dv), "wv_sb"], writes=[PB[6]],
                        inc=(dv == 3 and h == 7))
            em.op("act", lambda e, tc=tc: e.copy(
                out=vh_sb[:, tc, :, 0:64],
                in_=pb[6].ap()[:, 0:512].rearrange("p (h k) -> p h k", h=NH)),
                reads=[PB[6]], writes=[("vh", tc)])
        def qhkh(h):
            for gi, (g0, g1) in enumerate(BG):
                for dv in range(4):
                    em.op("pe", lambda e, h=h, dv=dv, g0=g0, g1=g1: e.matmul(
                        pb[6][0:64, 0:512], wq_sb[:, dv, h * 64:(h + 1) * 64],
                        q_sb[:, dv, g0:g1], start=(dv == 0), stop=(dv == 3)),
                        reads=[("q_sb", dv), "wq_sb"], writes=[PB[6]],
                        inc=(dv == 3))
                em.op("act", lambda e, h=h, g0=g0, g1=g1: e.activation(
                    out=qh_sb[0:64, g0:g1], in_=pb[6][0:64, 0:512],
                    func=AF.Identity, bias=bq_b[:, h:h + 1]),
                    reads=[PB[6], "bq_b"], writes=["qh"])
                for dv in range(4):
                    em.op("pe", lambda e, h=h, dv=dv, g0=g0, g1=g1: e.matmul(
                        pb[7][0:64, 0:512], wk_sb[:, dv, h * 64:(h + 1) * 64],
                        k_sb[:, dv, g0:g1], start=(dv == 0), stop=(dv == 3)),
                        reads=[("k_sb", dv), "wk_sb"], writes=[PB[7]],
                        inc=(dv == 3))
                em.op("act", lambda e, h=h, g0=g0, g1=g1: e.activation(
                    out=kh_sb[0:64, g0:g1], in_=pb[7][0:64, 0:512],
                    func=AF.Identity, bias=bk_b[:, h:h + 1]),
                    reads=[PB[7], "bk_b"], writes=["kh"])

        def kc_ranges(kc):
            rr = []
            for gi, (g0, g1) in enumerate(BG):
                r0 = max(g0, kc * 128)
                if r0 < g1:
                    rr.append((gi, r0, g1))
            return rr

        def scores_kc(h, kc):
            par = kc % 2
            for (gi, r0, r1) in kc_ranges(kc):
                bank = 2 * par + gi
                rw = r1 - r0
                em.op("pe", lambda e, kc=kc, r0=r0, r1=r1, rw=rw, bank=bank:
                      e.matmul(pb[bank][0:128, 0:rw],
                               kh_sb[0:64, kc * 128:(kc + 1) * 128],
                               qh_sb[0:64, r0:r1], start=True, stop=True),
                      reads=["kh", "qh"], writes=[PB[bank]])
                em.op("act", lambda e, kc=kc, r0=r0, r1=r1, rw=rw, bank=bank,
                      par=par: e.activation(
                    out=expT[par][:, r0:r1], in_=pb[bank][0:128, 0:rw],
                    func=AF.Exp, scale=ASCALE),
                    reads=[PB[bank]], writes=[("expT", par)])
            em.op("pool", lambda e, kc=kc, par=par: e.affine_select(
                out=expT[par][:, kc * 128:(kc + 1) * 128],
                in_=expT[par][:, kc * 128:(kc + 1) * 128],
                pattern=[[1, 128]], compare_op=ALU.is_ge, fill=0.0,
                base=0, channel_multiplier=-1),
                reads=[("expT", par)], writes=[("expT", par)])

        def ctx_kc(h, kc, ctx_started):
            par = kc % 2
            for (gi, r0, r1) in kc_ranges(kc):
                cbank = 4 + gi
                rw = r1 - r0
                last = (kc == 3) if gi == 0 else (kc == 7)
                em.op("pe", lambda e, kc=kc, h=h, gi=gi, r0=r0, r1=r1, rw=rw,
                      cbank=cbank, par=par, st=(not ctx_started[gi]),
                      last=last: e.matmul(
                    pb[cbank][0:65, r0 - 512 * gi:r1 - 512 * gi],
                    vh_sb[:, kc, h, :], expT[par][:, r0:r1],
                    start=st, stop=last),
                    reads=[("expT", par), ("vh", kc), "vh_ones"],
                    writes=[PB[cbank]], inc=last)
                ctx_started[gi] = True

        def ns_chain(h):
            for gi, (g0, g1) in enumerate(BG):
                cbank = 4 + gi
                nbank = gi
                em.op("dve", lambda e, gi=gi, g0=g0, cbank=cbank: e.reciprocal(
                    out=rows[0:1, g0:g0 + 512], in_=pb[cbank][64:65, 0:512]),
                    reads=[PB[cbank]], writes=[("nsr", gi)])
                em.op("pe", lambda e, gi=gi, g0=g0, nbank=nbank: e.matmul(
                    pb[nbank][0:64, 0:512], ones_r[0:1, 0:64],
                    rows[0:1, g0:g0 + 512], start=True, stop=True),
                    reads=[("nsr", gi), "ones_r"], writes=[PB[nbank]])
                em.op("act", lambda e, gi=gi, g0=g0, nbank=nbank: e.copy(
                    out=ns_sb[0:64, g0:g0 + 512], in_=pb[nbank][0:64, 0:512]),
                    reads=[PB[nbank]], writes=[("ns", gi)])
                po = 64 * (h % 2)
                em.op("dve", lambda e, h=h, gi=gi, g0=g0, g1=g1, cbank=cbank,
                      po=po: e.tensor_mul(
                    ctx_all[po:po + 64, h // 2, g0:g1], pb[cbank][0:64, 0:512],
                    ns_sb[0:64, g0:g1]),
                    reads=[PB[cbank], ("ns", gi)], writes=[("ctx", h // 2)])

        for h in range(NH):
            qhkh(h)
            if h > 0:
                ns_chain(h - 1)
            ctx_started = [False, False]
            prev = None
            for kc in range(8):
                scores_kc(h, kc)
                if prev is not None:
                    ctx_kc(h, prev, ctx_started)
                prev = kc
            ctx_kc(h, 7, ctx_started)
        ns_chain(NH - 1)
        for m in range(4):
            for gi, (g0, g1) in enumerate(BG):
                bank = (2 * m + gi) % 2
                for hd in range(4):
                    em.op("pe", lambda e, m=m, hd=hd, g0=g0, g1=g1, bank=bank:
                          e.matmul(pb[bank][0:128, 0:512],
                                   wo_sb[:, hd, m * 128:(m + 1) * 128],
                                   ctx_all[:, hd, g0:g1],
                                   start=(hd == 0), stop=(hd == 3)),
                          reads=[("ctx", hd), "wo_sb"], writes=[PB[bank]],
                          inc=(hd == 3))
                em.op("act", lambda e, m=m, g0=g0, g1=g1, bank=bank: e.activation(
                    out=mha_sb[:, m, g0:g1], in_=pb[bank][0:128, 0:512],
                    func=AF.Identity, bias=bo2_b[:, m:m + 1]),
                    reads=[PB[bank], "bo2_b"], writes=[("mha", m), ("q_sb", m)])
        pq_next = [0]

        def proj_fetch(upto):
            while pq_next[0] < min(upto, DC):
                i = pq_next[0]
                em.dma("dmaG", slab[i % 4].ap()[:, 0:4, :], projB.ap()[i],
                       writes=[("slab", i % 4)], stream=f"sl{i % 4}")
                pq_next[0] += 1

        for j in range(DC):
            proj_fetch(j + 4)
            ks = ("slab", j % 4)
            sl = slab[j % 4]
            for gi, (g0, g1) in enumerate(BG):
                bank = 2 + ((2 * j + gi) % 2)
                for kk in range(4):
                    em.op("pe", lambda e, j=j, kk=kk, g0=g0, g1=g1, bank=bank,
                          sl=sl: e.matmul(
                        pb[bank][0:128, 0:512], sl[:, kk, :], mha_sb[:, kk, g0:g1],
                        start=(kk == 0), stop=(kk == 3)),
                        reads=[ks, ("mha", kk)], writes=[PB[bank]], inc=(kk == 3))
                em.op("dve", lambda e, j=j, g0=g0, g1=g1, bank=bank:
                      e.scalar_tensor_tensor(
                    out=x_sb[:, j, g0 + 2:g1 + 2], in0=pb[bank][0:128, 0:512],
                    scalar=pb_b[:, j:j + 1], in1=x_sb[:, j, g0 + 2:g1 + 2],
                    op0=ALU.add, op1=ALU.add),
                    reads=[PB[bank], "pb_b", ("x", j)], writes=[("x", j)])

    for c in range(nchunk):
        a0 = 0 if c == 0 else 2
        for dc in range(DC):
            em.dma("dmaS", x_sb[:, dc, a0:WF],
                   xT.ap()[dc * 128:(dc + 1) * 128, c * W + a0:c * W + WF],
                   writes=[("x", dc)], stream=f"xl{dc}")
        if "ffn1" in phases:
            ln(a0, WF)
            ffn(a0, WF)
        if "attn" in phases:
            ln(a0, WF)
            if c > 0:
                em.op("pool", lambda e: e.tensor_copy(
                    out=hx[:, :, 0:2], in_=halo_sb[:, :, :]),
                    reads=["halo"], writes=[("hx", dc2) for dc2 in range(DC)])
            em.op("pool", lambda e: e.tensor_copy(
                out=halo_sb[:, :, :], in_=hx[:, :, 1024:1026]),
                reads=[("hx", dc2) for dc2 in range(DC)], writes=["halo"])
            em.fence()
            attn(c)
            em.fence()
        if "ffn3" in phases:
            ln(2, WF)
            ffn(2, WF)
        for dc in range(DC):
            em.dma("dmaS", outT.ap()[dc * 128:(dc + 1) * 128, c * W:(c + 1) * W],
                   x_sb[:, dc, 2:WF], reads=[("x", dc)], stream=f"os{dc}")

    for dc in range(DC):
        em.barrier_track("sp", f"os{dc}")
    print("BUILD stats:", dict(em.stats))
    return nc


_PERM = []
for a in range(HC):
    _PERM += [a, HC + a]


def prep_shared(inputs):
    gup = np.asarray(inputs["L0_ffn_gate_up_proj"], np.float32)
    down = np.asarray(inputs["L0_ffn_down_proj"], np.float32)
    conv_w = np.asarray(inputs["L0_attn_conv_w"], np.float32)
    proj_w = np.asarray(inputs["L0_attn_proj_w"], np.float32)
    router = np.asarray(inputs["L0_ffn_router_w"], np.float32)
    gupB = np.ascontiguousarray(
        gup.reshape(NE, 2 * HC, 128, DC, 128).transpose(0, 1, 4, 3, 2)[:, _PERM])
    downB = np.ascontiguousarray(
        down.reshape(NE, DC, 128, HC, 128).transpose(0, 1, 4, 3, 2))
    convB = np.ascontiguousarray(
        conv_w.reshape(3, DC, 128, QC, 128).transpose(0, 3, 2, 1, 4))
    projB = np.ascontiguousarray(
        proj_w.reshape(4, 128, DC, 128).transpose(2, 1, 0, 3))
    shared = dict(
        gupB=gupB, downB=downB, convB=convB, projB=projB,
        routT=np.ascontiguousarray(router.T),
        wq2=np.ascontiguousarray(np.asarray(inputs["L0_attn_wq"], np.float32).reshape(DV, 512)),
        wk2=np.ascontiguousarray(np.asarray(inputs["L0_attn_wk"], np.float32).reshape(DV, 512)),
        wv2=np.ascontiguousarray(np.asarray(inputs["L0_attn_wv"], np.float32).reshape(DV, 512)),
        wo2=np.ascontiguousarray(np.asarray(inputs["L0_attn_wo"], np.float32).reshape(DV, 512)),
        cbias=np.ascontiguousarray(np.asarray(inputs["L0_attn_conv_b"], np.float32).reshape(QC, 128).T),
        pbias=np.ascontiguousarray(np.asarray(inputs["L0_attn_proj_b"], np.float32).reshape(DC, 128).T),
        bqc=np.ascontiguousarray(np.asarray(inputs["L0_attn_bq"], np.float32).reshape(NH, 64).T),
        bkc=np.ascontiguousarray(np.asarray(inputs["L0_attn_bk"], np.float32).reshape(NH, 64).T),
        bvc=np.ascontiguousarray(np.asarray(inputs["L0_attn_bv"], np.float32).reshape(4, 128).T),
        boc=np.ascontiguousarray(np.asarray(inputs["L0_attn_bo"], np.float32).reshape(4, 128).T),
    )
    return shared


def prep_xT(inputs, core):
    x = np.asarray(inputs["L0_attn_x"], np.float32).reshape(TOK, D)
    s = core * CORE_T
    sl = np.empty((CORE_T + 2, D), np.float32)
    if s % 16384 == 0:
        sl[0:2] = 0.0
    else:
        sl[0:2] = x[s - 2:s]
    sl[2:] = x[s:s + CORE_T]
    return np.ascontiguousarray(sl.T)


_NC_CACHE = {}


def kernel(**inputs):
    key = "full"
    if key not in _NC_CACHE:
        _NC_CACHE[key] = build_program()
    nc = _NC_CACHE[key]
    shared = prep_shared(inputs)
    in_maps = [dict(xT=prep_xT(inputs, c), **shared) for c in range(8)]
    res = run_bass_kernel_spmd(nc, in_maps, core_ids=list(range(8)))
    out = np.empty((TOK, D), np.float32)
    for c in range(8):
        out[c * CORE_T:(c + 1) * CORE_T] = np.asarray(res.results[c]["outT"]).T
    return out.reshape(2, 16384, D)


# version 04
# speedup vs baseline: 1.0705x; 1.0705x over previous
import numpy as np
from collections import defaultdict

import concourse.bass as bass
import concourse.mybir as mybir
from concourse.bass_utils import run_bass_kernel_spmd

F32 = mybir.dt.float32
BF16 = mybir.dt.bfloat16
AF = mybir.ActivationFunctionType
ALU = mybir.AluOpType
AX = mybir.AxisListType

D = 2048
DC = 16
HC = 11
NE = 8
NH = 8
DK = 64
DV = 512
QC = 12
W = 1024
NCHUNK = 4
CORE_T = W * NCHUNK
TOK = 2 * 16384
EPS = 1e-5
ASCALE = 0.125
WF = 1026


def _groups(c0, c1, gmax=512):
    w = c1 - c0
    n = -(-w // gmax)
    out, base = [], c0
    for i in range(n):
        gw = (w + n - 1 - i) // n
        out.append((base, base + gw))
        base += gw
    assert base == c1
    return out


def _tiles128(c0, c1):
    out, b = [], c0
    while b < c1:
        out.append((b, min(b + 128, c1)))
        b = min(b + 128, c1)
    return out


class Emitter:
    COMPUTE = ("pe", "act", "dve", "pool", "sp")

    def __init__(self, nc):
        self.nc = nc
        self.eng = {"pe": nc.tensor, "act": nc.scalar, "dve": nc.vector,
                    "pool": nc.gpsimd, "sp": nc.sync}
        self.dma_issuer = {"dmaS": "sp", "dmaG": "pool", "dmaA": "act"}
        self.sem = {}
        self.count = defaultdict(int)
        for t in self.COMPUTE:
            self.sem[t] = nc.alloc_semaphore(f"sem_{t}")
        self.known = {e: defaultdict(int) for e in self.COMPUTE}
        self.writer = {}
        self.readers = {}
        self.stats = defaultdict(int)

    def _need(self, eng, track, val):
        if val <= 0 or self.known[eng][track] >= val:
            return
        self.eng[eng].wait_ge(self.sem[track], val)
        self.stats["waits"] += 1
        self.known[eng][track] = val

    def _dep_in(self, eng, reads, writes, pe_exempt):
        for k in reads:
            w = self.writer.get(k)
            if w is not None:
                t, v = w
                if t == eng == "pe" and pe_exempt:
                    continue
                if t == "pe" and v > self.count["pe"]:
                    raise RuntimeError(f"read {k}: PE val {v} not inc'd")
                self._need(eng, t, v)
        for k in writes:
            w = self.writer.get(k)
            if w is not None:
                t, v = w
                if not (t == eng == "pe" and pe_exempt):
                    if t == "pe" and v > self.count["pe"]:
                        raise RuntimeError(f"WAW {k}: PE val {v} not inc'd")
                    self._need(eng, t, v)
            for t, v in self.readers.get(k, {}).items():
                if t == eng == "pe" and pe_exempt:
                    continue
                if t == "pe" and v > self.count["pe"]:
                    raise RuntimeError(f"WAR {k}: PE val {v} not inc'd")
                self._need(eng, t, v)

    def _dep_out(self, track, myval, reads, writes):
        for k in writes:
            self.writer[k] = (track, myval)
            self.readers[k] = {}
        for k in reads:
            self.readers.setdefault(k, {})
            prev = self.readers[k].get(track, 0)
            self.readers[k][track] = max(prev, myval)

    def op(self, eng, emit, reads=(), writes=(), inc=True, pe_exempt=True):
        self._dep_in(eng, reads, writes, pe_exempt)
        ins = emit(self.eng[eng])
        self.stats["ops_" + eng] += 1
        if inc:
            self.count[eng] += 1
            ins.then_inc(self.sem[eng], 1)
            myval = self.count[eng]
        else:
            assert eng == "pe"
            myval = self.count[eng] + 1
        self._dep_out(eng, myval, reads, writes)
        return ins

    def dma(self, queue, out_ap, in_ap, reads=(), writes=(), stream=None,
            group_val=None, **kw):
        issuer = self.dma_issuer[queue]
        assert stream is not None
        if stream not in self.sem:
            self.sem[stream] = self.nc.alloc_semaphore(f"sem_{stream}")
        self._dep_in(issuer, reads, writes, False)
        ins = self.eng[issuer].dma_start(out=out_ap, in_=in_ap, **kw)
        self.count[stream] += 16
        ins.then_inc(self.sem[stream], 16)
        self.stats["dma_" + stream[:2]] += 1
        v = self.count[stream] if group_val is None else group_val
        self._dep_out(stream, v, reads, writes)
        return ins

    def barrier_track(self, eng, track, val=None):
        v = self.count[track] if val is None else val
        self._need(eng, track, v)

    def fence(self):
        tracks = [t for t in self.count.keys()]
        for e in self.COMPUTE:
            for t in tracks:
                if t != e:
                    self._need(e, t, self.count[t])
        self.stats["fences"] += 1


def build_program(phases=("ffn1", "attn", "ffn3"), nchunk=NCHUNK):
    nc = bass.Bass("TRN2", target_bir_lowering=False)

    xT = nc.dram_tensor("xT", [D, CORE_T + 2], F32, kind="ExternalInput")
    gupB = nc.dram_tensor("gupB", [NE, 2 * HC, 128, DC, 128], F32, kind="ExternalInput")
    downB = nc.dram_tensor("downB", [NE, DC, 128, HC, 128], F32, kind="ExternalInput")
    convB = nc.dram_tensor("convB", [3, QC, 128, DC, 128], F32, kind="ExternalInput")
    projB = nc.dram_tensor("projB", [DC, 128, 4, 128], F32, kind="ExternalInput")
    routT = nc.dram_tensor("routT", [D, NE], F32, kind="ExternalInput")
    wq2 = nc.dram_tensor("wq2", [DV, 512], F32, kind="ExternalInput")
    wk2 = nc.dram_tensor("wk2", [DV, 512], F32, kind="ExternalInput")
    wv2 = nc.dram_tensor("wv2", [DV, 512], F32, kind="ExternalInput")
    wo2 = nc.dram_tensor("wo2", [DV, 512], F32, kind="ExternalInput")
    cbias = nc.dram_tensor("cbias", [128, QC], F32, kind="ExternalInput")
    pbias = nc.dram_tensor("pbias", [128, DC], F32, kind="ExternalInput")
    bqc = nc.dram_tensor("bqc", [64, NH], F32, kind="ExternalInput")
    bkc = nc.dram_tensor("bkc", [64, NH], F32, kind="ExternalInput")
    bvc = nc.dram_tensor("bvc", [128, 4], F32, kind="ExternalInput")
    boc = nc.dram_tensor("boc", [128, 4], F32, kind="ExternalInput")
    outT = nc.dram_tensor("outT", [D, CORE_T], F32, kind="ExternalOutput")
    gupC = nc.dram_tensor("gupC", [NE, 2 * HC, 128, DC, 128], BF16, kind="Internal")
    downC = nc.dram_tensor("downC", [NE, DC, 128, HC, 128], BF16, kind="Internal")
    convC = nc.dram_tensor("convC", [3, QC, 128, DC, 128], BF16, kind="Internal")
    cache_state = {"gup": False, "down": False, "conv": False}

    x_sb = nc.alloc_sbuf_tensor("x_sb", [128, DC, WF], F32)
    hx = nc.alloc_sbuf_tensor("hx", [128, DC, WF], BF16)
    slab = [nc.alloc_sbuf_tensor(f"slab{i}", [128, DC, 128], BF16) for i in range(4)]
    dslab = [nc.alloc_sbuf_tensor(f"dslab{i}", [128, HC, 128], BF16) for i in range(2)]
    wq_sb = nc.alloc_sbuf_tensor("wq_sb", [128, 4, 512], BF16)
    wk_sb = nc.alloc_sbuf_tensor("wk_sb", [128, 4, 512], BF16)
    wv_sb = nc.alloc_sbuf_tensor("wv_sb", [128, 4, 512], BF16)
    wo_sb = nc.alloc_sbuf_tensor("wo_sb", [128, 4, 512], BF16)
    rout_sb = nc.alloc_sbuf_tensor("rout_sb", [128, DC, NE], BF16)
    cb_b = nc.alloc_sbuf_tensor("cb_b", [128, QC], F32)
    pb_b = nc.alloc_sbuf_tensor("pb_b", [128, DC], F32)
    bq_b = nc.alloc_sbuf_tensor("bq_b", [64, NH], F32)
    bk_b = nc.alloc_sbuf_tensor("bk_b", [64, NH], F32)
    bv_bf = nc.alloc_sbuf_tensor("bv_bf", [128, 4], BF16)
    bo_b = nc.alloc_sbuf_tensor("bo_b", [128, 4], F32)
    bo2_b = nc.alloc_sbuf_tensor("bo2_b", [128, 4], F32)
    ones_c = nc.alloc_sbuf_tensor("ones_c", [128, 1], BF16)
    ones_r = nc.alloc_sbuf_tensor("ones_r", [1, 128], F32)
    identf = nc.alloc_sbuf_tensor("identf", [128, 128], F32)
    onehot = nc.alloc_sbuf_tensor("onehot", [8, NE, 128], F32)
    eps_t = nc.alloc_sbuf_tensor("eps_t", [1, 1], F32)
    rows = nc.alloc_sbuf_tensor("rows", [1, 2048], F32)
    halo_sb = nc.alloc_sbuf_tensor("halo_sb", [128, DC, 2], BF16)
    sq_sb = [nc.alloc_sbuf_tensor(f"sq{i}", [128, 512], BF16) for i in range(2)]
    lg_sb = nc.alloc_sbuf_tensor("lg_sb", [128, 9, 8], F32)
    comb_sb = nc.alloc_sbuf_tensor("comb_sb", [128, 9, 8], F32)
    wcols = nc.alloc_sbuf_tensor("wcols", [128, 6], F32)
    eqs = nc.alloc_sbuf_tensor("eqs", [128, 3, 8], F32)
    q_sb = nc.alloc_sbuf_tensor("q_sb", [128, 4, W], BF16)
    k_sb = nc.alloc_sbuf_tensor("k_sb", [128, 4, W], BF16)
    t_sb = nc.alloc_sbuf_tensor("t_sb", [128, HC, WF], BF16)
    silu_sb = [nc.alloc_sbuf_tensor(f"silu{i}", [128, 512], F32) for i in range(3)]
    cb_sb = nc.alloc_sbuf_tensor("cb_sb", [128, 1056], F32)
    lgfm = nc.alloc_sbuf_tensor("lgfm", [8, 1056], F32)
    comb_fm = nc.alloc_sbuf_tensor("comb_fm", [8, 1056], F32)
    _base = nc.lookup_mloc(t_sb).addr
    _end = nc.lookup_mloc(comb_fm).addr + 1056 * 4

    def _at(name, shape, dtype, off):
        nonlocal_end = off + int(np.prod(shape[1:])) * mybir.dt.size(dtype)
        assert _base + nonlocal_end <= _end, (name, nonlocal_end, _end - _base)
        return nc.alloc_sbuf_tensor_at(name, shape, dtype, offset=_base + off)

    v_sb = _at("v_sb", [128, 4, W], BF16, 0)
    vh_sb = _at("vh_sb", [128, 8, NH, 65], BF16, 8192)
    qh_sb = _at("qh_sb", [64, W], BF16, 16512)
    kh_sb = _at("kh_sb", [64, W], BF16, 18560)
    expT = [_at(f"expT{i}", [128, W], BF16, 20608 + 2048 * i) for i in range(2)]
    ctx_all = _at("ctx_all", [128, 4, W], BF16, 24704)
    ns_sb = _at("ns_sb", [64, W], F32, 32896)
    mha_sb = nc.alloc_sbuf_tensor_at("mha_sb", [128, 4, W], BF16,
                                     offset=nc.lookup_mloc(q_sb).addr)

    pb = [nc.alloc_psum_tensor(f"pb{i}", [128, 512], F32) for i in range(8)]
    PB = [f"pb{i}" for i in range(8)]

    em = Emitter(nc)

    em.op("dve", lambda e: e.memset(ones_c[:, :], 1.0), writes=["ones_c"])
    em.op("dve", lambda e: e.memset(ones_r[:, :], 1.0), writes=["ones_r"])
    em.op("dve", lambda e: e.memset(eps_t[:, :], EPS), writes=["eps_t"])
    em.op("pool", lambda e: e.memset(identf[:, :], 1.0), writes=["identf"])
    em.op("pool", lambda e: e.affine_select(out=identf[:, :], in_=identf[:, :],
            pattern=[[-1, 128]], compare_op=ALU.is_equal, fill=0.0, base=0,
            channel_multiplier=1), reads=["identf"], writes=["identf"])
    em.op("pool", lambda e: e.memset(onehot.ap().rearrange("p a b -> p (a b)"), 1.0),
          writes=["onehot"])
    em.op("pool", lambda e: e.affine_select(
            out=onehot.ap().rearrange("p a b -> p a b"),
            in_=onehot.ap().rearrange("p a b -> p a b"),
            pattern=[[-1, NE], [0, 128]], compare_op=ALU.is_equal, fill=0.0,
            base=0, channel_multiplier=1), reads=["onehot"], writes=["onehot"])
    em.dma("dmaG", wq_sb.ap(),
           wq2.ap().rearrange("(a p) b -> p a b", p=128), writes=["wq_sb"],
           stream="initG", group_val=96)
    em.dma("dmaG", wk_sb.ap(),
           wk2.ap().rearrange("(a p) b -> p a b", p=128), writes=["wk_sb"],
           stream="initG", group_val=96)
    em.dma("dmaG", wv_sb.ap(),
           wv2.ap().rearrange("(a p) b -> p a b", p=128), writes=["wv_sb"],
           stream="initG", group_val=96)
    em.dma("dmaG", wo_sb.ap(),
           wo2.ap().rearrange("(a p) b -> p a b", p=128), writes=["wo_sb"],
           stream="initG", group_val=96)
    em.dma("dmaG", rout_sb.ap(),
           routT.ap().rearrange("(a p) b -> p a b", p=128), writes=["rout_sb"],
           stream="initG", group_val=96)
    em.dma("dmaG", bv_bf[:, :], bvc.ap(), writes=["bv_bf"],
           stream="initG", group_val=96)
    em.dma("dmaS", cb_b[:, :], cbias.ap(), writes=["cb_b"],
           stream="initS", group_val=80)
    em.dma("dmaS", pb_b[:, :], pbias.ap(), writes=["pb_b"],
           stream="initS", group_val=80)
    em.dma("dmaS", bq_b[:, :], bqc.ap(), writes=["bq_b"],
           stream="initS", group_val=80)
    em.dma("dmaS", bk_b[:, :], bkc.ap(), writes=["bk_b"],
           stream="initS", group_val=80)
    em.dma("dmaS", bo_b[:, :], boc.ap(), writes=["bo_b"],
           stream="initS", group_val=80)

    for m in range(4):
        for kk in range(4):
            em.op("pe", lambda e, m=m, kk=kk: e.matmul(
                pb[6][0:128, m:m + 1], wo_sb[:, kk, m * 128:(m + 1) * 128],
                bv_bf[:, kk:kk + 1], start=(m == 0 and kk == 0),
                stop=(m == 3 and kk == 3)),
                reads=["wo_sb", "bv_bf"], writes=[PB[6]],
                inc=(m == 3 and kk == 3))
    em.op("dve", lambda e: e.tensor_add(bo2_b[:, :], bo_b[:, :], pb[6][0:128, 0:4]),
          reads=["bo_b", PB[6]], writes=["bo2_b"])

    def ln(f0, f1):
        gs = _groups(f0, f1)
        for gi, (g0, g1) in enumerate(gs):
            gw = g1 - g0
            bs, bq = 2 * gi, 2 * gi + 1
            for dc in range(DC):
                em.op("act", lambda e, dc=dc, g0=g0, g1=g1: e.copy(
                    out=hx[:, dc, g0:g1], in_=x_sb[:, dc, g0:g1]),
                    reads=[("x", dc)], writes=[("hx", dc)])
                em.op("pool", lambda e, dc=dc, g0=g0, g1=g1, gw=gw: e.tensor_mul(
                    sq_sb[dc % 2][:, 0:gw], hx[:, dc, g0:g1], hx[:, dc, g0:g1]),
                    reads=[("hx", dc)], writes=[("sq", dc % 2)])
                em.op("pe", lambda e, dc=dc, g0=g0, g1=g1, gw=gw, bs=bs: e.matmul(
                    pb[bs][0:1, 0:gw], ones_c[:, :], hx[:, dc, g0:g1],
                    start=(dc == 0), stop=(dc == DC - 1)),
                    reads=[("hx", dc), "ones_c"], writes=[PB[bs]])
                em.op("pe", lambda e, dc=dc, gw=gw, bq=bq: e.matmul(
                    pb[bq][0:1, 0:gw], ones_c[:, :], sq_sb[dc % 2][:, 0:gw],
                    start=(dc == 0), stop=(dc == DC - 1)),
                    reads=[("sq", dc % 2), "ones_c"], writes=[PB[bq]])
            em.op("act", lambda e, gw=gw, bs=bs: e.mul(
                rows[0:1, 0:gw], pb[bs][0:1, 0:gw], 1.0 / D),
                reads=[PB[bs]], writes=["rA"])
            em.op("act", lambda e, gw=gw, bq=bq: e.mul(
                rows[0:1, 512:512 + gw], pb[bq][0:1, 0:gw], 1.0 / D),
                reads=[PB[bq]], writes=["rB"])
            em.op("dve", lambda e, gw=gw: e.tensor_mul(rows[0:1, 1024:1024 + gw],
                  rows[0:1, 0:gw], rows[0:1, 0:gw]), reads=["rA"], writes=["rC"])
            em.op("dve", lambda e, gw=gw: e.tensor_sub(rows[0:1, 512:512 + gw],
                  rows[0:1, 512:512 + gw], rows[0:1, 1024:1024 + gw]),
                  reads=["rB", "rC"], writes=["rB"])
            em.op("act", lambda e, gw=gw: e.activation(out=rows[0:1, 1024:1024 + gw],
                  in_=rows[0:1, 512:512 + gw], func=AF.Ln, bias=eps_t[0:1, :]),
                  reads=["rB", "eps_t"], writes=["rC"])
            em.op("act", lambda e, gw=gw: e.activation(out=rows[0:1, 1024:1024 + gw],
                  in_=rows[0:1, 1024:1024 + gw], func=AF.Exp, scale=-0.5),
                  reads=["rC"], writes=["rC"])
            em.op("pe", lambda e, gw=gw, bs=bs: e.matmul(pb[bs][0:128, 0:gw],
                  ones_r[:, :], rows[0:1, 0:gw], start=True, stop=True),
                  reads=["rA", "ones_r"], writes=[PB[bs]])
            em.op("pe", lambda e, gw=gw, bq=bq: e.matmul(pb[bq][0:128, 0:gw],
                  ones_r[:, :], rows[0:1, 1024:1024 + gw], start=True, stop=True),
                  reads=["rC", "ones_r"], writes=[PB[bq]])
        for gi, (g0, g1) in enumerate(gs):
            gw = g1 - g0
            bs, bq = 2 * gi, 2 * gi + 1
            for dc in range(DC):
                em.op("dve", lambda e, dc=dc, g0=g0, g1=g1, gw=gw, bs=bs: e.tensor_sub(
                    hx[:, dc, g0:g1], x_sb[:, dc, g0:g1], pb[bs][0:128, 0:gw]),
                    reads=[("x", dc), PB[bs]], writes=[("hx", dc)])
                em.op("dve", lambda e, dc=dc, g0=g0, g1=g1, gw=gw, bq=bq: e.tensor_mul(
                    hx[:, dc, g0:g1], hx[:, dc, g0:g1], pb[bq][0:128, 0:gw]),
                    reads=[("hx", dc), PB[bq]], writes=[("hx", dc)])

    def ffn(f0, f1):
        gs = _groups(f0, f1)
        ng = len(gs)
        tiles = _tiles128(f0, f1)

        def router():
            for gi, (g0, g1) in enumerate(gs):
                gw = g1 - g0
                for kk in range(DC):
                    em.op("pe", lambda e, kk=kk, g0=g0, g1=g1, gw=gw: e.matmul(
                        pb[6][0:8, 0:gw], rout_sb[:, kk, :], hx[:, kk, g0:g1],
                        start=(kk == 0), stop=(kk == DC - 1)),
                        reads=[("hx", kk), "rout_sb"], writes=[PB[6]],
                        inc=(kk == DC - 1))
                em.op("act", lambda e, g0=g0, g1=g1, gw=gw: e.copy(
                    out=lgfm[0:8, g0:g1], in_=pb[6][0:8, 0:gw]),
                    reads=[PB[6]], writes=["lgfm"])

        def routing_tiles():
            for ti, (t0, t1) in enumerate(tiles):
                tw = t1 - t0
                em.op("pe", lambda e, t0=t0, t1=t1, tw=tw: e.transpose(
                    pb[7][0:tw, 0:8], lgfm[0:8, t0:t1], identf[0:8, 0:8]),
                    reads=["lgfm", "identf"], writes=[PB[7]])
                em.op("act", lambda e, ti=ti, tw=tw: e.copy(
                    out=lg_sb[0:tw, ti, :], in_=pb[7][0:tw, 0:8]),
                    reads=[PB[7]], writes=[("lg", ti)])
                P = slice(0, tw)
                em.op("dve", lambda e, ti=ti, P=P: e.reduce_max(
                    out=wcols[P, 0:1], in_=lg_sb[P, ti, :], axis=AX.X),
                    reads=[("lg", ti)], writes=["wc"])
                em.op("dve", lambda e, ti=ti, P=P: e.tensor_scalar(
                    out=eqs[P, 0, :], in0=lg_sb[P, ti, :], scalar1=wcols[P, 0:1],
                    scalar2=None, op0=ALU.is_equal),
                    reads=[("lg", ti), "wc"], writes=["eq0"])
                em.op("dve", lambda e, P=P: e.tensor_scalar(
                    out=eqs[P, 1, :], in0=eqs[P, 0, :], scalar1=-2000.0,
                    scalar2=None, op0=ALU.mult), reads=["eq0"], writes=["eq1"])
                em.op("dve", lambda e, ti=ti, P=P: e.tensor_add(
                    eqs[P, 1, :], eqs[P, 1, :], lg_sb[P, ti, :]),
                    reads=[("lg", ti), "eq1"], writes=["eq1"])
                em.op("dve", lambda e, P=P: e.reduce_max(
                    out=wcols[P, 1:2], in_=eqs[P, 1, :], axis=AX.X),
                    reads=["eq1"], writes=["wc"])
                em.op("dve", lambda e, P=P: e.tensor_sub(
                    wcols[P, 2:3], wcols[P, 1:2], wcols[P, 0:1]),
                    reads=["wc"], writes=["wc"])
                em.op("act", lambda e, P=P: e.activation(
                    out=wcols[P, 2:3], in_=wcols[P, 2:3], func=AF.Exp),
                    reads=["wc"], writes=["wc"])
                em.op("act", lambda e, P=P: e.add(wcols[P, 3:4], wcols[P, 2:3], 1.0),
                      reads=["wc"], writes=["wc"])
                em.op("dve", lambda e, P=P: e.reciprocal(
                    out=wcols[P, 4:5], in_=wcols[P, 3:4]), reads=["wc"], writes=["wc"])
                em.op("dve", lambda e, P=P: e.tensor_scalar(
                    out=wcols[P, 5:6], in0=wcols[P, 4:5], scalar1=0.5, scalar2=None,
                    op0=ALU.mult), reads=["wc"], writes=["wc"])
                em.op("dve", lambda e, P=P: e.tensor_mul(
                    wcols[P, 2:3], wcols[P, 2:3], wcols[P, 5:6]),
                    reads=["wc"], writes=["wc"])
                em.op("dve", lambda e, P=P: e.tensor_scalar(
                    out=eqs[P, 2, :], in0=eqs[P, 1, :], scalar1=wcols[P, 1:2],
                    scalar2=None, op0=ALU.is_equal), reads=["eq1", "wc"], writes=["eq2"])
                em.op("dve", lambda e, P=P: e.tensor_scalar(
                    out=eqs[P, 0, :], in0=eqs[P, 0, :], scalar1=wcols[P, 5:6],
                    scalar2=None, op0=ALU.mult), reads=["eq0", "wc"], writes=["eq0"])
                em.op("dve", lambda e, P=P: e.tensor_scalar(
                    out=eqs[P, 2, :], in0=eqs[P, 2, :], scalar1=wcols[P, 2:3],
                    scalar2=None, op0=ALU.mult), reads=["eq2", "wc"], writes=["eq2"])
                em.op("dve", lambda e, ti=ti, P=P: e.tensor_add(
                    comb_sb[P, ti, :], eqs[P, 0, :], eqs[P, 2, :]),
                    reads=["eq0", "eq2"], writes=[("comb", ti)])

        def comb_transposes():
            for ti, (t0, t1) in enumerate(tiles):
                tw = t1 - t0
                em.op("pe", lambda e, ti=ti, tw=tw: e.transpose(
                    pb[7][0:8, 0:tw], comb_sb[0:tw, ti, :], identf[0:tw, 0:tw]),
                    reads=[("comb", ti), "identf"], writes=[PB[7]])
                em.op("act", lambda e, t0=t0, t1=t1, tw=tw: e.copy(
                    out=comb_fm[0:8, t0:t1], in_=pb[7][0:8, 0:tw]),
                    reads=[PB[7]], writes=["comb_fm"])

        def cb_bcast(e_, cbanks):
            for gi, (g0, g1) in enumerate(gs):
                gw = g1 - g0
                cbk = cbanks[gi]
                em.op("pe", lambda e, e_=e_, g0=g0, g1=g1, gw=gw, cbk=cbk: e.matmul(
                    pb[cbk][0:128, 0:gw], onehot[:, e_, :], comb_fm[0:8, g0:g1],
                    start=True, stop=True),
                    reads=["comb_fm", "onehot"], writes=[PB[cbk]])
                em.op("act", lambda e, g0=g0, g1=g1, gw=gw, cbk=cbk: e.copy(
                    out=cb_sb[:, g0:g1], in_=pb[cbk][0:128, 0:gw]),
                    reads=[PB[cbk]], writes=["cb"])

        gq = [(e_, jj) for e_ in range(NE) for jj in range(2 * HC)]
        gq_next = [0]
        gup_fill = not cache_state["gup"]
        cache_state["gup"] = True

        def gup_fetch(upto):
            while gq_next[0] < min(upto, len(gq)):
                i = gq_next[0]
                e_, jj = gq[i]
                if gup_fill:
                    em.dma("dmaG", slab[i % 4].ap(), gupB.ap()[e_, jj],
                           writes=[("slab", i % 4)], stream=f"sl{i % 4}")
                    em.dma("dmaS", gupC.ap()[e_, jj], slab[i % 4].ap(),
                           reads=[("slab", i % 4)], stream=f"cw{i % 4}")
                else:
                    em.dma("dmaS", slab[i % 4].ap(), gupC.ap()[e_, jj],
                           writes=[("slab", i % 4)], stream=f"sh{i % 4}")
                gq_next[0] += 1

        dq = [(e_, m) for e_ in range(NE) for m in range(DC)]
        dq_next = [0]
        down_fill = not cache_state["down"]
        cache_state["down"] = True

        def down_fetch(upto):
            while dq_next[0] < min(upto, len(dq)):
                i = dq_next[0]
                e_, m = dq[i]
                if down_fill:
                    em.dma("dmaG", dslab[i % 2].ap(), downB.ap()[e_, m],
                           writes=[("dslab", i % 2)], stream=f"dsl{i % 2}")
                    em.dma("dmaS", downC.ap()[e_, m], dslab[i % 2].ap(),
                           reads=[("dslab", i % 2)], stream=f"dw{i % 2}")
                else:
                    em.dma("dmaS", dslab[i % 2].ap(), downC.ap()[e_, m],
                           writes=[("dslab", i % 2)], stream=f"dh{i % 2}")
                dq_next[0] += 1

        def gu_banks(j2):
            if ng == 3:
                return [0, 1, 2], [3, 4, 5]
            if j2 % 2 == 0:
                return [0, 1], [2, 3]
            return [4, 5], [6, 7]

        def y_banks(m):
            if ng == 3:
                return [0, 1, 2]
            return [0, 1] if m % 2 == 0 else [2, 3]

        cbanks = [7, 7, 7] if ng == 3 else [6, 7]

        def gup_mms(e_, j2):
            ig = e_ * 2 * HC + 2 * j2
            gup_fetch(ig + 4)
            sg_, su_ = slab[ig % 4], slab[(ig + 1) % 4]
            kg, ku = ("slab", ig % 4), ("slab", (ig + 1) % 4)
            gb, ub = gu_banks(j2)
            for gi, (g0, g1) in enumerate(gs):
                gw = g1 - g0
                for kk in range(DC):
                    em.op("pe", lambda e, kk=kk, g0=g0, g1=g1, gw=gw, bk=gb[gi],
                          sg_=sg_: e.matmul(
                        pb[bk][0:128, 0:gw], sg_[:, kk, :], hx[:, kk, g0:g1],
                        start=(kk == 0), stop=(kk == DC - 1)),
                        reads=[kg, ("hx", kk)], writes=[PB[gb[gi]]],
                        inc=(kk == DC - 1))
                for kk in range(DC):
                    em.op("pe", lambda e, kk=kk, g0=g0, g1=g1, gw=gw, bk=ub[gi],
                          su_=su_: e.matmul(
                        pb[bk][0:128, 0:gw], su_[:, kk, :], hx[:, kk, g0:g1],
                        start=(kk == 0), stop=(kk == DC - 1)),
                        reads=[ku, ("hx", kk)], writes=[PB[ub[gi]]],
                        inc=(kk == DC - 1))

        def silu_evac(j2):
            gb, ub = gu_banks(j2)
            for gi, (g0, g1) in enumerate(gs):
                gw = g1 - g0
                em.op("act", lambda e, gi=gi, gw=gw, bk=gb[gi]: e.activation(
                    out=silu_sb[gi][:, 0:gw], in_=pb[bk][0:128, 0:gw],
                    func=AF.Silu), reads=[PB[gb[gi]]], writes=[("silu", gi)])

        def t_evac(j2):
            gb, ub = gu_banks(j2)
            for gi, (g0, g1) in enumerate(gs):
                gw = g1 - g0
                em.op("dve", lambda e, gi=gi, g0=g0, g1=g1, gw=gw: e.tensor_mul(
                    silu_sb[gi][:, 0:gw], silu_sb[gi][:, 0:gw], cb_sb[:, g0:g1]),
                    reads=[("silu", gi), "cb"], writes=[("silu", gi)])
                em.op("dve", lambda e, j2=j2, gi=gi, g0=g0, g1=g1, gw=gw,
                      bk=ub[gi]: e.tensor_mul(
                    t_sb[:, j2, g0:g1], silu_sb[gi][:, 0:gw],
                    pb[bk][0:128, 0:gw]),
                    reads=[("silu", gi), PB[ub[gi]]], writes=[("t", j2)])

        def down_phase(e_):
            for m in range(DC):
                idn = e_ * DC + m
                down_fetch(idn + 2)
                ds_ = dslab[idn % 2]
                kd = ("dslab", idn % 2)
                yb = y_banks(m)
                for gi, (g0, g1) in enumerate(gs):
                    gw = g1 - g0
                    for k2 in range(HC):
                        em.op("pe", lambda e, k2=k2, g0=g0, g1=g1, gw=gw,
                              bk=yb[gi], ds_=ds_: e.matmul(
                            pb[bk][0:128, 0:gw], ds_[:, k2, :], t_sb[:, k2, g0:g1],
                            start=(k2 == 0), stop=(k2 == HC - 1)),
                            reads=[kd, ("t", k2)], writes=[PB[yb[gi]]],
                            inc=(k2 == HC - 1))
                    ybk = yb[gi]
                    em.op("dve", lambda e, m=m, g0=g0, g1=g1, gw=gw, bk=ybk:
                          e.tensor_add(x_sb[:, m, g0:g1], x_sb[:, m, g0:g1],
                                       pb[bk][0:128, 0:gw]),
                          reads=[("x", m), PB[ybk]], writes=[("x", m)])

        gup_fetch(4)
        router()
        routing_tiles()
        for e_ in range(NE):
            if e_ == 0 and ng == 2:
                gup_mms(0, 0)
                silu_evac(0)
                gup_mms(0, 1)
                comb_transposes()
                cb_bcast(0, [0, 1])
                t_evac(0)
                silu_evac(1)
                t_evac(1)
                start_j2 = 2
            else:
                if e_ == 0:
                    comb_transposes()
                cb_bcast(e_, cbanks)
                start_j2 = 0
            for j2 in range(start_j2, HC):
                gup_mms(e_, j2)
                silu_evac(j2)
                t_evac(j2)
            down_phase(e_)

    def attn(c):
        BG = [(0, 512), (512, 1024)]
        em.op("dve", lambda e: e.memset(vh_sb[:, :, :, 64:65], 1.0),
              writes=["vh_ones"])
        cq = [(w_, j) for j in range(QC) for w_ in range(3)]
        cq_next = [0]
        conv_fill = not cache_state["conv"]
        cache_state["conv"] = True

        def conv_fetch(upto):
            while cq_next[0] < min(upto, len(cq)):
                i = cq_next[0]
                w_, j = cq[i]
                if conv_fill:
                    em.dma("dmaG", slab[i % 4].ap(), convB.ap()[w_, j],
                           writes=[("slab", i % 4)], stream=f"sl{i % 4}")
                    em.dma("dmaS", convC.ap()[w_, j], slab[i % 4].ap(),
                           reads=[("slab", i % 4)], stream=f"cw{i % 4}")
                else:
                    em.dma("dmaS", slab[i % 4].ap(), convC.ap()[w_, j],
                           writes=[("slab", i % 4)], stream=f"sh{i % 4}")
                cq_next[0] += 1

        for j in range(QC):
            conv_fetch(3 * j + 4)
            for gi, (g0, g1) in enumerate(BG):
                bank = 2 * (j % 2) + gi
                first = True
                for w_ in range(3):
                    i = 3 * j + w_
                    ks = ("slab", i % 4)
                    sl = slab[i % 4]
                    for kk in range(DC):
                        em.op("pe", lambda e, kk=kk, w_=w_, g0=g0, bank=bank,
                              sl=sl, first=first: e.matmul(
                            pb[bank][0:128, 0:512], sl[:, kk, :],
                            hx[:, kk, g0 + w_:g0 + w_ + 512],
                            start=first, stop=(w_ == 2 and kk == DC - 1)),
                            reads=[ks, ("hx", kk)], writes=[PB[bank]],
                            inc=(w_ == 2 and kk == DC - 1))
                        first = False
                dst = (q_sb, j) if j < 4 else ((k_sb, j - 4) if j < 8 else (v_sb, j - 8))
                dt_, dj = dst
                dname = "q_sb" if j < 4 else ("k_sb" if j < 8 else "v_sb")
                em.op("act", lambda e, dt_=dt_, dj=dj, j=j, g0=g0, g1=g1,
                      bank=bank: e.activation(
                    out=dt_[:, dj, g0:g1], in_=pb[bank][0:128, 0:512],
                    func=AF.Identity, bias=cb_b[:, j:j + 1]),
                    reads=[PB[bank], "cb_b"], writes=[(dname, dj)])
        for tc in range(8):
            for dv in range(4):
                for h in range(NH):
                    em.op("pe", lambda e, tc=tc, dv=dv, h=h: e.matmul(
                        pb[6][0:128, h * 64:(h + 1) * 64],
                        v_sb[:, dv, tc * 128:(tc + 1) * 128],
                        wv_sb[:, dv, h * 64:(h + 1) * 64],
                        start=(dv == 0 and h == 0), stop=(dv == 3 and h == 7)),
                        reads=[("v_sb", dv), "wv_sb"], writes=[PB[6]],
                        inc=(dv == 3 and h == 7))
            em.op("act", lambda e, tc=tc: e.copy(
                out=vh_sb[:, tc, :, 0:64],
                in_=pb[6].ap()[:, 0:512].rearrange("p (h k) -> p h k", h=NH)),
                reads=[PB[6]], writes=[("vh", tc)])
        def qhkh(h):
            for gi, (g0, g1) in enumerate(BG):
                for dv in range(4):
                    em.op("pe", lambda e, h=h, dv=dv, g0=g0, g1=g1: e.matmul(
                        pb[6][0:64, 0:512], wq_sb[:, dv, h * 64:(h + 1) * 64],
                        q_sb[:, dv, g0:g1], start=(dv == 0), stop=(dv == 3)),
                        reads=[("q_sb", dv), "wq_sb"], writes=[PB[6]],
                        inc=(dv == 3))
                em.op("act", lambda e, h=h, g0=g0, g1=g1: e.activation(
                    out=qh_sb[0:64, g0:g1], in_=pb[6][0:64, 0:512],
                    func=AF.Identity, bias=bq_b[:, h:h + 1]),
                    reads=[PB[6], "bq_b"], writes=["qh"])
                for dv in range(4):
                    em.op("pe", lambda e, h=h, dv=dv, g0=g0, g1=g1: e.matmul(
                        pb[7][0:64, 0:512], wk_sb[:, dv, h * 64:(h + 1) * 64],
                        k_sb[:, dv, g0:g1], start=(dv == 0), stop=(dv == 3)),
                        reads=[("k_sb", dv), "wk_sb"], writes=[PB[7]],
                        inc=(dv == 3))
                em.op("act", lambda e, h=h, g0=g0, g1=g1: e.activation(
                    out=kh_sb[0:64, g0:g1], in_=pb[7][0:64, 0:512],
                    func=AF.Identity, bias=bk_b[:, h:h + 1]),
                    reads=[PB[7], "bk_b"], writes=["kh"])

        def kc_ranges(kc):
            rr = []
            for gi, (g0, g1) in enumerate(BG):
                r0 = max(g0, kc * 128)
                if r0 < g1:
                    rr.append((gi, r0, g1))
            return rr

        def scores_kc(h, kc):
            par = kc % 2
            for (gi, r0, r1) in kc_ranges(kc):
                bank = 2 * par + gi
                rw = r1 - r0
                em.op("pe", lambda e, kc=kc, r0=r0, r1=r1, rw=rw, bank=bank:
                      e.matmul(pb[bank][0:128, 0:rw],
                               kh_sb[0:64, kc * 128:(kc + 1) * 128],
                               qh_sb[0:64, r0:r1], start=True, stop=True),
                      reads=["kh", "qh"], writes=[PB[bank]])
                em.op("act", lambda e, kc=kc, r0=r0, r1=r1, rw=rw, bank=bank,
                      par=par: e.activation(
                    out=expT[par][:, r0:r1], in_=pb[bank][0:128, 0:rw],
                    func=AF.Exp, scale=ASCALE),
                    reads=[PB[bank]], writes=[("expT", par)])
            em.op("pool", lambda e, kc=kc, par=par: e.affine_select(
                out=expT[par][:, kc * 128:(kc + 1) * 128],
                in_=expT[par][:, kc * 128:(kc + 1) * 128],
                pattern=[[1, 128]], compare_op=ALU.is_ge, fill=0.0,
                base=0, channel_multiplier=-1),
                reads=[("expT", par)], writes=[("expT", par)])

        def ctx_kc(h, kc, ctx_started):
            par = kc % 2
            for (gi, r0, r1) in kc_ranges(kc):
                cbank = 4 + gi
                rw = r1 - r0
                last = (kc == 3) if gi == 0 else (kc == 7)
                em.op("pe", lambda e, kc=kc, h=h, gi=gi, r0=r0, r1=r1, rw=rw,
                      cbank=cbank, par=par, st=(not ctx_started[gi]),
                      last=last: e.matmul(
                    pb[cbank][0:65, r0 - 512 * gi:r1 - 512 * gi],
                    vh_sb[:, kc, h, :], expT[par][:, r0:r1],
                    start=st, stop=last),
                    reads=[("expT", par), ("vh", kc), "vh_ones"],
                    writes=[PB[cbank]], inc=last)
                ctx_started[gi] = True

        def ns_chain(h):
            for gi, (g0, g1) in enumerate(BG):
                cbank = 4 + gi
                nbank = gi
                em.op("dve", lambda e, gi=gi, g0=g0, cbank=cbank: e.reciprocal(
                    out=rows[0:1, g0:g0 + 512], in_=pb[cbank][64:65, 0:512]),
                    reads=[PB[cbank]], writes=[("nsr", gi)])
                em.op("pe", lambda e, gi=gi, g0=g0, nbank=nbank: e.matmul(
                    pb[nbank][0:64, 0:512], ones_r[0:1, 0:64],
                    rows[0:1, g0:g0 + 512], start=True, stop=True),
                    reads=[("nsr", gi), "ones_r"], writes=[PB[nbank]])
                em.op("act", lambda e, gi=gi, g0=g0, nbank=nbank: e.copy(
                    out=ns_sb[0:64, g0:g0 + 512], in_=pb[nbank][0:64, 0:512]),
                    reads=[PB[nbank]], writes=[("ns", gi)])
                po = 64 * (h % 2)
                em.op("dve", lambda e, h=h, gi=gi, g0=g0, g1=g1, cbank=cbank,
                      po=po: e.tensor_mul(
                    ctx_all[po:po + 64, h // 2, g0:g1], pb[cbank][0:64, 0:512],
                    ns_sb[0:64, g0:g1]),
                    reads=[PB[cbank], ("ns", gi)], writes=[("ctx", h // 2)])

        for h in range(NH):
            qhkh(h)
            if h > 0:
                ns_chain(h - 1)
            ctx_started = [False, False]
            prev = None
            for kc in range(8):
                scores_kc(h, kc)
                if prev is not None:
                    ctx_kc(h, prev, ctx_started)
                prev = kc
            ctx_kc(h, 7, ctx_started)
        ns_chain(NH - 1)
        for m in range(4):
            for gi, (g0, g1) in enumerate(BG):
                bank = (2 * m + gi) % 2
                for hd in range(4):
                    em.op("pe", lambda e, m=m, hd=hd, g0=g0, g1=g1, bank=bank:
                          e.matmul(pb[bank][0:128, 0:512],
                                   wo_sb[:, hd, m * 128:(m + 1) * 128],
                                   ctx_all[:, hd, g0:g1],
                                   start=(hd == 0), stop=(hd == 3)),
                          reads=[("ctx", hd), "wo_sb"], writes=[PB[bank]],
                          inc=(hd == 3))
                em.op("act", lambda e, m=m, g0=g0, g1=g1, bank=bank: e.activation(
                    out=mha_sb[:, m, g0:g1], in_=pb[bank][0:128, 0:512],
                    func=AF.Identity, bias=bo2_b[:, m:m + 1]),
                    reads=[PB[bank], "bo2_b"], writes=[("mha", m), ("q_sb", m)])
        pq_next = [0]

        def proj_fetch(upto):
            while pq_next[0] < min(upto, DC):
                i = pq_next[0]
                em.dma("dmaG", slab[i % 4].ap()[:, 0:4, :], projB.ap()[i],
                       writes=[("slab", i % 4)], stream=f"sl{i % 4}")
                pq_next[0] += 1

        for j in range(DC):
            proj_fetch(j + 4)
            ks = ("slab", j % 4)
            sl = slab[j % 4]
            for gi, (g0, g1) in enumerate(BG):
                bank = 2 + ((2 * j + gi) % 2)
                for kk in range(4):
                    em.op("pe", lambda e, j=j, kk=kk, g0=g0, g1=g1, bank=bank,
                          sl=sl: e.matmul(
                        pb[bank][0:128, 0:512], sl[:, kk, :], mha_sb[:, kk, g0:g1],
                        start=(kk == 0), stop=(kk == 3)),
                        reads=[ks, ("mha", kk)], writes=[PB[bank]], inc=(kk == 3))
                em.op("dve", lambda e, j=j, g0=g0, g1=g1, bank=bank:
                      e.scalar_tensor_tensor(
                    out=x_sb[:, j, g0 + 2:g1 + 2], in0=pb[bank][0:128, 0:512],
                    scalar=pb_b[:, j:j + 1], in1=x_sb[:, j, g0 + 2:g1 + 2],
                    op0=ALU.add, op1=ALU.add),
                    reads=[PB[bank], "pb_b", ("x", j)], writes=[("x", j)])

    for c in range(nchunk):
        a0 = 0 if c == 0 else 2
        for dc in range(DC):
            em.dma("dmaS", x_sb[:, dc, a0:WF],
                   xT.ap()[dc * 128:(dc + 1) * 128, c * W + a0:c * W + WF],
                   writes=[("x", dc)], stream=f"xl{dc}")
        if "ffn1" in phases:
            ln(a0, WF)
            ffn(a0, WF)
        if "attn" in phases:
            ln(a0, WF)
            if c > 0:
                em.op("pool", lambda e: e.tensor_copy(
                    out=hx[:, :, 0:2], in_=halo_sb[:, :, :]),
                    reads=["halo"], writes=[("hx", dc2) for dc2 in range(DC)])
            em.op("pool", lambda e: e.tensor_copy(
                out=halo_sb[:, :, :], in_=hx[:, :, 1024:1026]),
                reads=[("hx", dc2) for dc2 in range(DC)], writes=["halo"])
            em.fence()
            attn(c)
            em.fence()
        if "ffn3" in phases:
            ln(2, WF)
            ffn(2, WF)
        for dc in range(DC):
            em.dma("dmaS", outT.ap()[dc * 128:(dc + 1) * 128, c * W:(c + 1) * W],
                   x_sb[:, dc, 2:WF], reads=[("x", dc)], stream=f"os{dc}")

    for dc in range(DC):
        em.barrier_track("sp", f"os{dc}")
    print("BUILD stats:", dict(em.stats))
    return nc


_PERM = []
for a in range(HC):
    _PERM += [a, HC + a]


def prep_shared(inputs):
    gup = np.asarray(inputs["L0_ffn_gate_up_proj"], np.float32)
    down = np.asarray(inputs["L0_ffn_down_proj"], np.float32)
    conv_w = np.asarray(inputs["L0_attn_conv_w"], np.float32)
    proj_w = np.asarray(inputs["L0_attn_proj_w"], np.float32)
    router = np.asarray(inputs["L0_ffn_router_w"], np.float32)
    gupB = np.ascontiguousarray(
        gup.reshape(NE, 2 * HC, 128, DC, 128).transpose(0, 1, 4, 3, 2)[:, _PERM])
    downB = np.ascontiguousarray(
        down.reshape(NE, DC, 128, HC, 128).transpose(0, 1, 4, 3, 2))
    convB = np.ascontiguousarray(
        conv_w.reshape(3, DC, 128, QC, 128).transpose(0, 3, 2, 1, 4))
    projB = np.ascontiguousarray(
        proj_w.reshape(4, 128, DC, 128).transpose(2, 1, 0, 3))
    shared = dict(
        gupB=gupB, downB=downB, convB=convB, projB=projB,
        routT=np.ascontiguousarray(router.T),
        wq2=np.ascontiguousarray(np.asarray(inputs["L0_attn_wq"], np.float32).reshape(DV, 512)),
        wk2=np.ascontiguousarray(np.asarray(inputs["L0_attn_wk"], np.float32).reshape(DV, 512)),
        wv2=np.ascontiguousarray(np.asarray(inputs["L0_attn_wv"], np.float32).reshape(DV, 512)),
        wo2=np.ascontiguousarray(np.asarray(inputs["L0_attn_wo"], np.float32).reshape(DV, 512)),
        cbias=np.ascontiguousarray(np.asarray(inputs["L0_attn_conv_b"], np.float32).reshape(QC, 128).T),
        pbias=np.ascontiguousarray(np.asarray(inputs["L0_attn_proj_b"], np.float32).reshape(DC, 128).T),
        bqc=np.ascontiguousarray(np.asarray(inputs["L0_attn_bq"], np.float32).reshape(NH, 64).T),
        bkc=np.ascontiguousarray(np.asarray(inputs["L0_attn_bk"], np.float32).reshape(NH, 64).T),
        bvc=np.ascontiguousarray(np.asarray(inputs["L0_attn_bv"], np.float32).reshape(4, 128).T),
        boc=np.ascontiguousarray(np.asarray(inputs["L0_attn_bo"], np.float32).reshape(4, 128).T),
    )
    return shared


def prep_xT(inputs, core):
    x = np.asarray(inputs["L0_attn_x"], np.float32).reshape(TOK, D)
    s = core * CORE_T
    sl = np.empty((CORE_T + 2, D), np.float32)
    if s % 16384 == 0:
        sl[0:2] = 0.0
    else:
        sl[0:2] = x[s - 2:s]
    sl[2:] = x[s:s + CORE_T]
    return np.ascontiguousarray(sl.T)


_NC_CACHE = {}


def kernel(**inputs):
    key = "full"
    if key not in _NC_CACHE:
        _NC_CACHE[key] = build_program()
    nc = _NC_CACHE[key]
    shared = prep_shared(inputs)
    in_maps = [dict(xT=prep_xT(inputs, c), **shared) for c in range(8)]
    res = run_bass_kernel_spmd(nc, in_maps, core_ids=list(range(8)))
    out = np.empty((TOK, D), np.float32)
    for c in range(8):
        out[c * CORE_T:(c + 1) * CORE_T] = np.asarray(res.results[c]["outT"]).T
    return out.reshape(2, 16384, D)
